# version 02
# baseline (speedup 1.0000x reference)
import math
import numpy as np
import ml_dtypes
from contextlib import ExitStack, contextmanager

import concourse.bass as bass
import concourse.mybir as mybir
from concourse.bass_utils import run_bass_kernel_spmd

F32 = mybir.dt.float32
BF16 = mybir.dt.bfloat16
ALU = mybir.AluOpType
AF = mybir.ActivationFunctionType
AX = mybir.AxisListType.X

D = 2048
NH = 16
NCK = 16
NCORES = 8
CPB = 4

CFG_FULL = dict(T=2048, F=1408, FS=5632, L=8192)
CFG_SMALL = dict(T=256, F=128, FS=512, L=1024)


def compute_seq_shifts(num_shifts=16, max_len=2048):
    if num_shifts <= 1:
        return [0]
    half = (num_shifts - 1 + 1) // 2
    max_s = max(max_len // 2, 2)
    if half <= 1:
        positive = [1]
    else:
        positive = sorted(set(max(1, round(math.exp(math.log(max_s) * i / (half - 1)))) for i in range(half)))
        while len(positive) < half:
            new_vals = []
            for i in range(len(positive) - 1):
                mid = (positive[i] + positive[i + 1]) // 2
                if mid not in positive and mid not in new_vals:
                    new_vals.append(mid)
                if len(positive) + len(new_vals) >= half:
                    break
            if not new_vals:
                v = positive[-1] + 1
                while len(positive) + len(new_vals) < half and v <= max_s:
                    new_vals.append(v)
                    v += 1
            positive = sorted(set(positive + new_vals))[:half]
    shifts = [0]
    for s in positive:
        shifts.append(s)
        shifts.append(-s)
    return shifts[:num_shifts]


SHIFTS = compute_seq_shifts(16, 2048)
HALO_L = 1024
HALO_R = 384
NSH = len(SHIFTS)
SCALE0 = 128.0 ** -0.5
DEBUG_NO_CC = False
DEBUG_S3 = False
CC_DELAY = 0



ENGINES = ("tensor", "vector", "scalar", "gpsimd", "sync")


class Sem:
    __slots__ = ("h", "n")

    def __init__(self, h):
        self.h = h
        self.n = 0


class BB:

    def __init__(self, nc, ctx):
        self.nc = nc
        self.ctx = ctx
        self.streams = {e: [] for e in ENGINES}
        self._semnames = set()
        self.prog = {e: self.sem("prog_" + e) for e in ("vector", "scalar", "gpsimd", "tensor")}
        self.dS = self.sem("dmaS")
        self.dG = self.sem("dmaG")
        self.dmasems = {}
        self.bar_sem = self.sem("barrier")
        self._scopes = []

    def sem(self, name):
        while name in self._semnames:
            name = name + "_"
        self._semnames.add(name)
        return Sem(self.ctx.enter_context(self.nc.semaphore(name)))

    def sbuf(self, name, shape, dtype):
        stack = self._scopes[-1] if self._scopes else self.ctx
        return stack.enter_context(self.nc.sbuf_tensor(name, shape, dtype))

    def psum(self, name, shape, dtype):
        stack = self._scopes[-1] if self._scopes else self.ctx
        return stack.enter_context(self.nc.psum_tensor(name, shape, dtype))

    @contextmanager
    def scope(self):
        st = ExitStack()
        self._scopes.append(st)
        try:
            yield st
        finally:
            self.barrier()
            self._scopes.pop()
            st.close()

    @staticmethod
    def _resolve(wait):
        out = []
        for w in (wait or ()):
            if w is None:
                continue
            if isinstance(w, Sem):
                out.append((w, w.n))
            else:
                out.append((w[0], w[1]))
        return out

    def op(self, eng, meth, *args, wait=None, inc="auto", **kwargs):
        waits = self._resolve(wait)
        if inc == "auto":
            if eng in ("vector", "scalar"):
                incs = [(self.prog[eng], 1)]
            elif eng == "gpsimd" and meth not in ("dma_start",):
                incs = [(self.prog["gpsimd"], 1)]
            else:
                incs = []
        elif inc:
            incs = list(inc)
        else:
            incs = []
        tok = None
        for s, k in incs:
            s.n += k
            tok = (s, s.n)
        self.streams[eng].append((waits, meth, args, kwargs, incs))
        return tok

    def v(self, meth, *a, **kw):
        return self.op("vector", meth, *a, **kw)

    def s(self, meth, *a, **kw):
        return self.op("scalar", meth, *a, **kw)

    def t(self, meth, *a, **kw):
        kw.setdefault("inc", None)
        return self.op("tensor", meth, *a, **kw)

    def t_fin(self, meth, *a, **kw):
        kw["inc"] = [(self.prog["tensor"], 1)]
        return self.op("tensor", meth, *a, **kw)

    def g(self, meth, *a, **kw):
        return self.op("gpsimd", meth, *a, **kw)

    def dma(self, q, dst, src, wait=None, key=None, **kw):
        if key is None:
            try:
                dst_sbuf = "SBUF" in str(dst.space)
            except Exception:
                dst_sbuf = False
            try:
                src_sbuf = "SBUF" in str(src.space)
            except Exception:
                src_sbuf = False
            if dst_sbuf:
                key = dst.name
            elif src_sbuf:
                key = src.name
            else:
                key = dst.name
        sem = self.dmasems.get(key)
        if sem is None:
            sem = self.sem("dma_" + key)
            self.dmasems[key] = sem
        return self.op(q, "dma_start", wait=wait, inc=[(sem, 16)], out=dst, in_=src, **kw)

    def barrier(self):
        waits = [self.prog[e] for e in ("vector", "scalar", "gpsimd", "tensor")] + \
            list(self.dmasems.values())
        self.op("sync", "nop", wait=waits, inc=[(self.bar_sem, 1)])
        btok = (self.bar_sem, self.bar_sem.n)
        for e in ("tensor", "vector", "scalar", "gpsimd"):
            self.op(e, "nop", wait=[btok], inc=None)

    def finalize(self, block):
        for eng in ENGINES:
            stream = self.streams[eng]

            def emit(E, stream=stream):
                for waits, meth, args, kwargs, incs in stream:
                    for s, val in waits:
                        E.wait_ge(s.h, val)
                    ins = getattr(E, meth)(*args, **kwargs)
                    for s, k in incs:
                        ins.then_inc(s.h, k)
            getattr(block, eng)(emit)


def load_weight_split(b, dst_sb, src2d, kcn, inner, groups=4, dtype_elems=None):
    per = kcn // groups
    assert per * groups == kcn
    tok = None
    for g in range(groups):
        s3 = src2d[g * per * 128:(g + 1) * per * 128, :].rearrange("(kc p) n -> p kc n", p=128)
        dst = dst_sb[:, g * per * inner:(g + 1) * per * inner].rearrange("p (kc n) -> p kc n", n=inner)
        tok = b.dma("gpsimd", dst, s3)
    return tok



def mk_ln_scratch(b, tag, width):
    return dict(
        sq=b.sbuf(f"lnsq_{tag}", [128, width], F32),
        ss=b.sbuf(f"lnss_{tag}", [128, 1], F32),
        ms=b.sbuf(f"lnms_{tag}", [128, 1], F32),
        mean=b.sbuf(f"lnmean_{tag}", [128, 1], F32),
        var=b.sbuf(f"lnvar_{tag}", [128, 1], F32),
        msq=b.sbuf(f"lnmsq_{tag}", [128, 1], F32),
        rstd=b.sbuf(f"lnrstd_{tag}", [128, 1], F32),
    )


def ln_tile(b, out_ap, in_ap, scr, width, eps=1e-5, extra_tok=None, out_guard=None):
    w0 = list(extra_tok or [])
    t1 = b.v("reduce_sum", scr["ms"][:], in_ap, AX, wait=w0)
    t2 = b.s("activation", scr["sq"][:], in_ap, AF.Square, accum_out=scr["ss"][:], wait=w0)
    t3 = b.v("tensor_scalar", scr["mean"][:], scr["ms"][:], 1.0 / width, None, ALU.mult, wait=[t1])
    t4 = b.v("tensor_tensor", scr["msq"][:], scr["mean"][:], scr["mean"][:], ALU.mult, wait=[t3])
    t5 = b.v("tensor_scalar", scr["var"][:], scr["ss"][:], 1.0 / width, None, ALU.mult, wait=[t2, t4])
    t6 = b.v("tensor_tensor", scr["var"][:], scr["var"][:], scr["msq"][:], ALU.subtract, wait=[t5])
    t6 = b.v("tensor_scalar", scr["var"][:], scr["var"][:], float(eps), None, ALU.add, wait=[t6])
    t7 = b.s("activation", scr["var"][:], scr["var"][:], AF.Sqrt, wait=[t6])
    t8 = b.v("reciprocal", scr["rstd"][:], scr["var"][:], wait=[t7])
    t9 = b.v("tensor_scalar", out_ap, in_ap, scr["mean"][:, 0:1], scr["rstd"][:, 0:1],
             ALU.subtract, ALU.mult, wait=[t8] + list(out_guard or []))
    return t9


def transpose_tm_to_fm(b, psbf, src_bf, dst_sb_slices, ident, first_wait):
    evs = []
    for half in range(2):
        mm = None
        for jj in range(8):
            c = half * 8 + jj
            w = first_wait if (jj == 0) else None
            mm = b.t_fin("transpose", psbf[half][:, jj * 128:(jj + 1) * 128],
                         src_bf[:, c * 128:(c + 1) * 128], ident[:], wait=w)
        evs.append(b.v("tensor_copy", dst_sb_slices(half), psbf[half][:], wait=[mm]))
    return evs



def _np(x):
    return np.ascontiguousarray(np.asarray(x), dtype=np.float32) if not isinstance(x, np.ndarray) \
        else np.ascontiguousarray(x.astype(np.float32, copy=False))


def prep_host(inputs, cfg):
    T, F, FS, L = cfg["T"], cfg["F"], cfg["FS"], cfg["L"]
    Text = HALO_L + T + HALO_R
    x = _np(inputs["L0_attn_x"])
    sh = {}
    sh["s1_wqT"] = _np(inputs["L0_attn_wq"]).T.copy()
    sh["s1_wkT"] = _np(inputs["L0_attn_wk"]).T.copy()
    sh["s1_wvT"] = _np(inputs["L0_attn_wv"]).T.copy()
    sh["s1_woT"] = _np(inputs["L0_attn_wo"]).T.copy()
    for nm in ("bq", "bk", "bv", "bo", "gq", "gk"):
        sh["s1_" + nm] = _np(inputs["L0_attn_" + nm])
    sh["s1_wmix"] = _np(inputs["L0_attn_w_mix"])
    nsv = FS // F
    wg = _np(inputs["L0_ffn_Wg"])
    wu = _np(inputs["L0_ffn_Wu"])
    wd = _np(inputs["L0_ffn_Wd"])
    Sg = _np(inputs["L0_ffn_Sg"])
    Su = _np(inputs["L0_ffn_Su"])
    Sd = _np(inputs["L0_ffn_Sd"])
    sh["s2_wg"] = np.concatenate([wg] + [Sg[None, :, v * F:(v + 1) * F] for v in range(nsv)], 0).reshape(-1).copy()
    sh["s2_wu"] = np.concatenate([wu] + [Su[None, :, v * F:(v + 1) * F] for v in range(nsv)], 0).reshape(-1).copy()
    sh["s2_wd"] = np.concatenate([wd] + [Sd[None, v * F:(v + 1) * F, :] for v in range(nsv)], 0).reshape(-1).copy()
    sh["s2_rw9"] = np.concatenate([_np(inputs["L0_ffn_Wr"]), _np(inputs["L0_ffn_Wsg"])], 1).copy()
    for nm in ("Wq", "Wk", "Wv", "Wo", "Wg1", "Wg2"):
        sh["s3_" + nm] = _np(inputs["L1_attn_" + nm])
    sh["s3_normw_cols"] = _np(inputs["L1_attn_norm_w"]).reshape(NCK, 128).T.copy()
    sh["s3_ld"] = _np(inputs["L1_attn_log_decay"]).reshape(1, NH).copy()
    wg4 = _np(inputs["L1_ffn_w_gate"]).transpose(0, 2, 1)
    wu4 = _np(inputs["L1_ffn_w_up"]).transpose(0, 2, 1)
    wd4 = _np(inputs["L1_ffn_w_down"]).transpose(0, 2, 1)
    sh["s4_wg"] = np.concatenate([wg4, _np(inputs["L1_ffn_shared_gate"]).T[None]], 0).reshape(-1).copy()
    sh["s4_wu"] = np.concatenate([wu4, _np(inputs["L1_ffn_shared_up"]).T[None]], 0).reshape(-1).copy()
    sh["s4_wd"] = np.concatenate([wd4, _np(inputs["L1_ffn_shared_down"]).T[None]], 0).reshape(-1).copy()
    rw = _np(inputs["L1_ffn_router_w"]).T
    sh["s4_rw9"] = np.concatenate([rw, np.zeros((D, 1), np.float32)], 1).copy()
    sh["ident"] = np.eye(128, dtype=np.float32).astype(ml_dtypes.bfloat16)
    sh["ident_rep"] = np.concatenate([np.eye(128, dtype=np.float32)] * NH, axis=1).astype(ml_dtypes.bfloat16)
    col = np.arange(128)[None, :].astype(np.float32)
    row = np.arange(128)[:, None].astype(np.float32)
    sh["diffmat"] = np.where(col >= row, col - row, 0.0).astype(np.float32)
    sh["maskmat"] = (col >= row).astype(np.float32)
    sh["iq"] = (np.arange(128, dtype=np.float32) + 1.0).reshape(1, 128)
    sh["ik"] = (127.0 - np.arange(128, dtype=np.float32)).reshape(1, 128)
    sh["ikcol"] = (127.0 - np.arange(128, dtype=np.float32)).reshape(128, 1)
    cores = []
    for j in range(NCORES):
        bidx, slot = j // CPB, j % CPB
        r0 = slot * T
        rows = (r0 - HALO_L + np.arange(Text)) % L
        m = dict(sh)
        m["x_ext"] = x[bidx][rows].copy()
        dmat = np.zeros((1, 8), np.float32)
        mask8 = np.zeros((1, 8), np.float32)
        for jp in range(NCORES):
            bp, sp = jp // CPB, jp % CPB
            if bp == bidx and sp < slot:
                dmat[0, jp] = float(T * (slot - 1 - sp))
                mask8[0, jp] = 1.0
        m["dmat128"] = np.tile(dmat, (1, 16)).copy()
        m["mask128"] = np.tile(mask8, (1, 16)).copy()
        cores.append(m)
    return sh, cores



def emit_stage1(b, H, cfg):
    T = cfg["T"]
    Text = HALO_L + T + HALO_R
    n_ext = Text // 128
    n_t = T // 128

    with b.scope():
        ident = b.sbuf("ident1a", [128, 128], BF16)
        id_tok = b.dma("sync", ident[:], H["ident"][:, :])
        xt = [b.sbuf(f"s1xt{i}", [128, D], F32) for i in range(2)]
        xbf = b.sbuf("s1xbf", [128, D], BF16)
        fmst = [b.sbuf(f"s1fm{i}", [128, D], BF16) for i in range(2)]
        psbf = [b.psum(f"s1pst{i}", [128, 1024], BF16) for i in range(2)]
        scr = mk_ln_scratch(b, "s1a", D)
        ld_t = [None, None]
        ln_free = [None, None]
        out_t = [None, None]
        for i in range(n_ext):
            sl = i % 2
            ld_t[sl] = b.dma("sync", xt[sl][:], H["x_ext"][i * 128:(i + 1) * 128, :],
                             wait=[ln_free[sl]])
            tln = ln_tile(b, xbf[:], xt[sl][:], scr, D, extra_tok=[ld_t[sl]])
            ln_free[sl] = tln
            evs = transpose_tm_to_fm(
                b, psbf, xbf,
                lambda half, sl=sl: fmst[sl][:, half * 1024:(half + 1) * 1024],
                ident, [tln, out_t[sl], id_tok])
            out_t[sl] = b.dma("sync",
                              H["xlnf1"][i * 128 * D:(i + 1) * 128 * D].rearrange("(p f) -> p f", p=128),
                              fmst[sl][:], wait=evs)

    def proj_pass(wname, out_scr, tiles, kind):
        with b.scope():
            wsb = b.sbuf("w_" + wname, [128, NCK * D], BF16)
            wtok = load_weight_split(b, wsb, H[wname][:, :], NCK, D, groups=4)
            bias = b.sbuf("bias_" + wname, [128, D], F32)
            bnm = dict(k="s1_bk", v="s1_bv", q="s1_bq")[kind]
            btok = b.dma("sync", bias[:], H[bnm][None, :].broadcast_to([128, D]))
            gain = None
            gtok = None
            if kind in ("k", "q"):
                gain = b.sbuf("gain_" + wname, [128, D], F32)
                gnm = "s1_gk" if kind == "k" else "s1_gq"
                gtok = b.dma("sync", gain[:], H[gnm][None, :].broadcast_to([128, D]))
            wmixc = None
            wmtok = None
            if kind == "q":
                wmixc = b.sbuf("wmixc", [128, 5], F32)
                tw = b.dma("sync", wmixc[:], H["s1_wmix"][0:1, :].broadcast_to([128, 5]))
                wmtok = b.v("tensor_scalar", wmixc[:], wmixc[:], SCALE0, None, ALU.mult, wait=[tw])
            fmt = [b.sbuf(f"fmt_{wname}{i}", [128, D], BF16) for i in range(2)]
            ps = [b.psum(f"pp_{wname}{i}", [128, D], F32) for i in range(2)]
            of = b.sbuf(f"of_{wname}", [128, D], F32)
            on = b.sbuf(f"on_{wname}", [128, D], F32)
            ob = [b.sbuf(f"ob_{wname}{i}", [128, D], BF16) for i in range(2)]
            scr = mk_ln_scratch(b, "rms_" + wname, D)
            ld_t = [None, None]
            fm_free = [None, None]
            ps_free = [None, None]
            ob_free = [None, None]
            for idx, i in enumerate(tiles):
                sl = idx % 2
                ld_t[sl] = b.dma("sync", fmt[sl][:],
                                 H["xlnf1"][i * 128 * D:(i + 1) * 128 * D].rearrange("(p f) -> p f", p=128),
                                 wait=[fm_free[sl]])
                pp = ps[sl]
                mmtok = None
                for nsub in range(4):
                    for kc in range(NCK):
                        w = None
                        if nsub == 0 and kc == 0:
                            w = [ld_t[sl], wtok, ps_free[sl]]
                        mmtok = b.t_fin("matmul", pp[:, nsub * 512:(nsub + 1) * 512],
                                        fmt[sl][:, kc * 128:(kc + 1) * 128],
                                        wsb[:, kc * D + nsub * 512: kc * D + (nsub + 1) * 512],
                                        start=(kc == 0), stop=(kc == NCK - 1), wait=w)
                fm_free[sl] = mmtok
                tb = b.v("tensor_tensor", of[:], pp[:], bias[:], ALU.add, wait=[mmtok, btok])
                ps_free[sl] = tb
                if kind == "v":
                    last = b.v("tensor_copy", ob[sl][:], of[:], wait=[tb, ob_free[sl]])
                else:
                    t2 = b.s("activation", scr["sq"][:], of[:], AF.Square, accum_out=scr["ss"][:], wait=[tb])
                    t3 = b.v("tensor_scalar", scr["var"][:], scr["ss"][:], 1.0 / D, 1e-6,
                             ALU.mult, ALU.add, wait=[t2])
                    t4 = b.s("activation", scr["var"][:], scr["var"][:], AF.Sqrt, wait=[t3])
                    t5 = b.v("reciprocal", scr["rstd"][:], scr["var"][:], wait=[t4])
                    t6 = b.v("scalar_tensor_tensor", on[:], of[:], scr["rstd"][:, 0:1], gain[:],
                             ALU.mult, ALU.mult, wait=[t5, gtok])
                    if kind == "k":
                        last = b.v("tensor_copy", ob[sl][:], on[:], wait=[t6, ob_free[sl]])
                    else:
                        on3 = on[:].rearrange("p (h d) -> p h d", d=128)
                        ob3 = ob[sl][:].rearrange("p (h d) -> p h d", d=128)
                        tm = b.v("tensor_scalar", ob[sl][:], on[:], wmixc[:, 0:1], None, ALU.mult,
                                 wait=[t6, wmtok, ob_free[sl]])
                        for ci, c in enumerate((1, 2, 4, 8)):
                            tm = b.v("scalar_tensor_tensor", ob3[:, :, c:128], on3[:, :, 0:128 - c],
                                     wmixc[:, ci + 1:ci + 2], ob3[:, :, c:128], ALU.mult, ALU.add,
                                     wait=[tm])
                            tm = b.v("scalar_tensor_tensor", ob3[:, :, 0:c], on3[:, :, 128 - c:128],
                                     wmixc[:, ci + 1:ci + 2], ob3[:, :, 0:c], ALU.mult, ALU.add,
                                     wait=[tm])
                        last = tm
                row0 = (i if kind != "q" else idx) * 128
                ob_free[sl] = b.dma("sync", out_scr[row0:row0 + 128, :], ob[sl][:], wait=[last])

    proj_pass("s1_wkT", H["k1"], list(range(n_ext)), "k")
    proj_pass("s1_wvT", H["v1"], list(range(n_ext)), "v")
    proj_pass("s1_wqT", H["q1"], [8 + i for i in range(n_t)], "q")

    with b.scope():
        idrep = b.sbuf("idrep", [128, D], BF16)
        irtok = b.dma("sync", idrep[:], H["ident_rep"][:, :])
        qm = b.sbuf("qm1", [128, D], BF16)
        kw = b.sbuf("kw1", [128, D], BF16)
        vw = [b.sbuf(f"vw1_{s}", [128, D], BF16) for s in range(NSH)]
        prod = b.sbuf("prod1", [128, D], BF16)
        scores = b.sbuf("scores1", [128, NH * NSH], F32)
        mx = b.sbuf("mx1", [128, NH], F32)
        smv = b.sbuf("sm1", [128, NH], F32)
        attn = b.sbuf("attn1", [128, NH * NSH], F32)
        slab = [b.sbuf(f"slab1_{i}", [128, D], BF16) for i in range(2)]
        avp = b.psum("avp1", [128, D], F32)
        ao = b.sbuf("ao1s", [128, D], BF16)
        qtok = None
        ktok = None
        vtoks = [None] * NSH
        sc_toks = [None] * NSH
        av_evac = None
        ao_dma = None
        slab_free = [None, None]
        last_prod = None
        attn_free = None
        for i in range(n_t):
            t0e = HALO_L + i * 128
            qtok = b.dma("sync", qm[:], H["q1"][i * 128:(i + 1) * 128, :], wait=[last_prod])
            for si, shf in enumerate(SHIFTS):
                r0 = t0e - shf
                ktok = b.dma("sync", kw[:], H["k1"][r0:r0 + 128, :], wait=[last_prod])
                tp = b.v("tensor_tensor", prod[:], qm[:], kw[:], ALU.mult,
                         wait=[ktok, qtok, sc_toks[si - 1] if si else (sc_toks[-1] if i else None)])
                last_prod = tp
                sc_out = scores[:].rearrange("p (h s) -> p h s", s=NSH)[:, :, si:si + 1]
                sc_toks[si] = b.v("tensor_reduce", sc_out,
                                  prod[:].rearrange("p (h d) -> p h d", d=128), AX, ALU.add,
                                  wait=[tp, attn_free if si == 0 else None])
                vtoks[si] = b.dma("sync", vw[si][:], H["v1"][r0:r0 + 128, :],
                                  wait=[vtoks[si], av_evac])
            s3 = scores[:].rearrange("p (h s) -> p h s", s=NSH)
            a3 = attn[:].rearrange("p (h s) -> p h s", s=NSH)
            t1 = b.v("tensor_reduce", mx[:], s3, AX, ALU.max, wait=sc_toks)
            t2 = b.v("tensor_tensor", a3, s3, mx[:].unsqueeze(2).broadcast_to([128, NH, NSH]),
                     ALU.subtract, wait=[t1, slab_free[0], slab_free[1]])
            t3 = b.s("activation", attn[:], attn[:], AF.Exp, wait=[t2])
            t4 = b.v("tensor_reduce", smv[:], a3, AX, ALU.add, wait=[t3])
            t5 = b.v("reciprocal", smv[:], smv[:], wait=[t4])
            t6 = b.v("tensor_tensor", a3, a3, smv[:].unsqueeze(2).broadcast_to([128, NH, NSH]),
                     ALU.mult, wait=[t5])
            attn_free = t6
            mmtok = None
            for si in range(NSH):
                sl = si % 2
                bc = attn[:].rearrange("p (h s) -> p h s", s=NSH)[:, :, si:si + 1].broadcast_to([128, NH, 128])
                ts_ = b.v("tensor_tensor", slab[sl][:].rearrange("p (h d) -> p h d", d=128),
                          idrep[:].rearrange("p (h d) -> p h d", d=128), bc, ALU.mult,
                          wait=[t6, slab_free[sl], irtok if (i == 0 and si < 2) else None])
                for h in range(NH):
                    w = [ts_, vtoks[si]]
                    if si == 0 and h % 4 == 0:
                        w.append(av_evac)
                    mmtok = b.t_fin("matmul", avp[:, h * 128:(h + 1) * 128],
                                    slab[sl][:, h * 128:(h + 1) * 128],
                                    vw[si][:, h * 128:(h + 1) * 128],
                                    start=(si == 0 and h % 4 == 0),
                                    stop=(si == NSH - 1 and h % 4 == 3),
                                    wait=w)
                slab_free[sl] = mmtok
            av_evac = b.s("activation", ao[:], avp[:], AF.Copy, wait=[mmtok, ao_dma])
            ao_dma = b.dma("sync", H["ao1"][i * 128:(i + 1) * 128, :], ao[:], wait=[av_evac])

    with b.scope():
        ident = b.sbuf("ident1e", [128, 128], BF16)
        idtok = b.dma("sync", ident[:], H["ident"][:, :])
        wsb = b.sbuf("w_s1_woT", [128, NCK * D], BF16)
        wtok = load_weight_split(b, wsb, H["s1_woT"][:, :], NCK, D, groups=4)
        bob = b.sbuf("bob", [128, D], F32)
        botok = b.dma("sync", bob[:], H["s1_bo"][None, :].broadcast_to([128, D]))
        aot = [b.sbuf(f"aot{i}", [128, D], BF16) for i in range(2)]
        aof = b.sbuf("aof", [128, D], BF16)
        psbf = [b.psum(f"ps1e{i}", [128, 1024], BF16) for i in range(2)]
        pp = b.psum("pp1e", [128, D], F32)
        of = b.sbuf("of1e", [128, D], F32)
        lno = b.sbuf("lno1e", [128, D], F32)
        xo = b.sbuf("xo1e", [128, D], F32)
        scr = mk_ln_scratch(b, "s1e", D)
        atok = [None, None]
        afree = [None, None]
        pfree = None
        xfree = None
        outdma = None
        for i in range(n_t):
            sl = i % 2
            atok[sl] = b.dma("sync", aot[sl][:], H["ao1"][i * 128:(i + 1) * 128, :],
                             wait=[afree[sl]])
            evs = transpose_tm_to_fm(b, psbf, aot[sl][:],
                                     lambda half: aof[:, half * 1024:(half + 1) * 1024],
                                     ident, [atok[sl], idtok, pfree])
            afree[sl] = evs[-1]
            mmtok = None
            for nsub in range(4):
                for kc in range(NCK):
                    w = None
                    if nsub == 0 and kc == 0:
                        w = evs + [wtok, pfree]
                    mmtok = b.t_fin("matmul", pp[:, nsub * 512:(nsub + 1) * 512],
                                    aof[:, kc * 128:(kc + 1) * 128],
                                    wsb[:, kc * D + nsub * 512: kc * D + (nsub + 1) * 512],
                                    start=(kc == 0), stop=(kc == NCK - 1), wait=w)
            tb = b.v("tensor_tensor", of[:], pp[:], bob[:], ALU.add, wait=[mmtok, botok])
            pfree = tb
            tln = ln_tile(b, lno[:], of[:], scr, D, extra_tok=[tb, outdma])
            xtok = b.dma("sync", xo[:], H["x_ext"][(8 + i) * 128:(9 + i) * 128, :], wait=[xfree])
            tr = b.v("tensor_tensor", lno[:], lno[:], xo[:], ALU.add, wait=[tln, xtok])
            xfree = tr
            outdma = b.dma("sync", H["x1"][i * 128:(i + 1) * 128, :], lno[:], wait=[tr])



def emit_moe(b, H, cfg, stage):
    T, F, FS = cfg["T"], cfg["F"], cfg["FS"]
    n_t = T // 128
    icc = F // 128
    nE = 8 + (FS // F if stage == 2 else 1)
    act = AF.Silu if stage == 2 else AF.Sigmoid
    pre = f"s{stage}_"
    x_in = H["x1"] if stage == 2 else H["x3"]
    x_out = H["x2"] if stage == 2 else H["xout"]
    h_scr = H[f"h{stage}"]
    TG = T // 2
    ntg = n_t // 2
    W = min(512, TG)
    nsub_t = T // W
    EW = D * F

    with b.scope():
        ident = b.sbuf(f"ident{stage}", [128, 128], BF16)
        idtok = b.dma("sync", ident[:], H["ident"][:, :])
        rw = b.sbuf(f"rw{stage}", [128, NCK * 9], BF16)
        rwtok = None
        C_all = b.sbuf(f"Call{stage}", [128, n_t * 9], F32)
        ab_stack = ExitStack()
        b._scopes.append(ab_stack)

        xlnf = b.sbuf(f"xlnf{stage}", [128, NCK * T], BF16)
        xt = [b.sbuf(f"xt{stage}_{i}", [128, D], F32) for i in range(2)]
        xbf = b.sbuf(f"xbf{stage}", [128, D], BF16)
        xlo = b.sbuf(f"xlo{stage}", [128, D], BF16)
        xlof = b.sbuf(f"xlof{stage}", [128, D], BF16)
        xl32 = b.sbuf(f"xl32_{stage}", [128, D], F32)
        rwf = b.sbuf(f"rwf{stage}", [128, NCK * 9], F32)
        rwlo = b.sbuf(f"rwlo{stage}", [128, NCK * 9], BF16)
        rwftok = b.dma("sync", rwf[:].rearrange("p (kc n) -> p kc n", n=9),
                       H[pre + "rw9"][:, :].rearrange("(kc p) n -> p kc n", p=128))
        rwhitok = b.v("tensor_copy", rw[:], rwf[:], wait=[rwftok, rwtok])
        rwlotok = b.v("tensor_tensor", rwlo[:], rwf[:], rw[:], ALU.subtract, wait=[rwhitok])
        psbf = [b.psum(f"pst{stage}_{i}", [128, 1024], BF16) for i in range(2)]
        ps9 = b.psum(f"ps9_{stage}", [128, 512], F32)
        prb = b.sbuf(f"pr{stage}", [128, 8], F32)
        m1 = b.sbuf(f"m1_{stage}", [128, 1], F32)
        mk1 = b.sbuf(f"mk1_{stage}", [128, 8], F32)
        pmb = b.sbuf(f"pm_{stage}", [128, 8], F32)
        mk2 = b.sbuf(f"mk2_{stage}", [128, 8], F32)
        scr = mk_ln_scratch(b, f"s{stage}pre", D)
        ldt = [None, None]
        fret = [None, None]
        r9free = None
        xldma = None
        xl_guard = None
        bf_guard = None
        lo_guard = None
        lof_guard = None
        for i in range(n_t):
            sl = i % 2
            ldt[sl] = b.dma("sync", xt[sl][:], x_in[i * 128:(i + 1) * 128, :], wait=[fret[sl]])
            tln = ln_tile(b, xl32[:], xt[sl][:], scr, D, extra_tok=[ldt[sl]],
                          out_guard=[xldma, xl_guard])
            fret[sl] = tln
            tbf = b.v("tensor_copy", xbf[:], xl32[:], wait=[tln, bf_guard])
            tlo = b.v("tensor_tensor", xlo[:], xl32[:], xbf[:], ALU.subtract, wait=[tbf, lo_guard])
            if stage == 4:
                xldma = b.dma("sync", H["xln4"][i * 128:(i + 1) * 128, :], xl32[:], wait=[tlo])
            else:
                xl_guard = tlo
            evs = transpose_tm_to_fm(
                b, psbf, xbf,
                lambda half, i=i: xlnf[:].rearrange("p (c t) -> p c t", t=T)[:, half * 8:(half + 1) * 8,
                                                                            i * 128:(i + 1) * 128],
                ident, [tbf, idtok])
            bf_guard = evs[-1]
            evs_lo = transpose_tm_to_fm(
                b, psbf, xlo,
                lambda half: xlof[:, half * 1024:(half + 1) * 1024],
                ident, [tlo, evs[-1], lof_guard])
            lo_guard = evs_lo[-1]
            mm = None
            for grp, (lh, rh) in enumerate(((xlnf, rw), (xlof, rw), (xlnf, rwlo))):
                for kc in range(NCK):
                    w = None
                    if grp == 0 and kc == 0:
                        w = evs + evs_lo + [rwlotok, r9free]
                    lslice = (lh[:, kc * T + i * 128: kc * T + i * 128 + 128] if lh is xlnf
                              else lh[:, kc * 128:(kc + 1) * 128])
                    mm = b.t_fin("matmul", ps9[:, 0:9], lslice, rh[:, kc * 9:(kc + 1) * 9],
                                 start=(grp == 0 and kc == 0), stop=(grp == 2 and kc == NCK - 1),
                                 wait=w)
            lof_guard = mm
            t1 = b.v("tensor_reduce", m1[:], ps9[:, 0:8], AX, ALU.max, wait=[mm])
            t2 = b.v("tensor_tensor", prb[:], ps9[:, 0:8], m1[:, 0:1].broadcast_to([128, 8]),
                     ALU.subtract, wait=[t1])
            t3 = b.s("activation", prb[:], prb[:], AF.Exp, wait=[t2])
            t4 = b.v("tensor_reduce", m1[:], prb[:], AX, ALU.add, wait=[t3])
            t5 = b.v("reciprocal", m1[:], m1[:], wait=[t4])
            t6 = b.v("tensor_scalar", prb[:], prb[:], m1[:, 0:1], None, ALU.mult, wait=[t5])
            t7 = b.v("tensor_reduce", m1[:], prb[:], AX, ALU.max, wait=[t6])
            t8 = b.v("tensor_tensor", mk1[:], prb[:], m1[:, 0:1].broadcast_to([128, 8]), ALU.is_ge, wait=[t7])
            t9 = b.v("scalar_tensor_tensor", pmb[:], mk1[:], -1e30, prb[:], ALU.mult, ALU.add, wait=[t8])
            t10 = b.v("tensor_reduce", m1[:], pmb[:], AX, ALU.max, wait=[t9])
            t11 = b.v("tensor_tensor", mk2[:], pmb[:], m1[:, 0:1].broadcast_to([128, 8]), ALU.is_ge, wait=[t10])
            t12 = b.v("tensor_tensor", mk1[:], mk1[:], mk2[:], ALU.add, wait=[t11])
            t13 = b.v("tensor_tensor", C_all[:, i * 9:i * 9 + 8], mk1[:], prb[:], ALU.mult, wait=[t12])
            if stage == 2:
                r9free = b.s("activation", C_all[:, i * 9 + 8:i * 9 + 9], ps9[:, 8:9], AF.Sigmoid, wait=[t13])
            else:
                r9free = b.v("memset", C_all[:, i * 9 + 8:i * 9 + 9], 1.0, wait=[t13])

        wbuf = b.sbuf(f"wup{stage}", [128, NCK * F], BF16)
        gsil = b.sbuf(f"gsil{stage}", [128, icc * T], BF16)
        hst = [b.sbuf(f"hst{stage}_{i}", [128, W], BF16) for i in range(2)]
        pup = [b.psum(f"pup{stage}_{i}", [128, W], F32) for i in range(2)]
        wg_d = H[pre + "wg"]
        wu_d = H[pre + "wu"]
        wfree = None
        pfree = [None, None]
        hst_free = [None, None]
        for e in range(nE):
            g_evs = {}
            for phase in range(2):
                src = wg_d if phase == 0 else wu_d
                wtoks = []
                grp = 4 if NCK % 4 == 0 else 1
                per = NCK // grp
                for gg in range(grp):
                    s2 = src[e * EW + gg * per * 128 * F: e * EW + (gg + 1) * per * 128 * F]
                    wtoks.append(b.dma("gpsimd",
                                       wbuf[:, gg * per * F:(gg + 1) * per * F].rearrange("p (kc n) -> p kc n", n=F),
                                       s2.rearrange("(kc p n) -> p kc n", p=128, n=F),
                                       wait=[wfree] if gg == 0 else None))
                wtok = wtoks[-1]
                lastev = None
                for ic in range(icc):
                    for tsub in range(nsub_t):
                        sl = (ic * nsub_t + tsub) % 2
                        mm = None
                        for kc in range(NCK):
                            w = None
                            if kc == 0:
                                w = [wtok, pfree[sl]]
                            mm = b.t_fin("matmul", pup[sl][:],
                                         wbuf[:, kc * F + ic * 128: kc * F + ic * 128 + 128],
                                         xlnf[:, kc * T + tsub * W: kc * T + (tsub + 1) * W],
                                         start=(kc == 0), stop=(kc == NCK - 1), wait=w)
                        if phase == 0:
                            ev = b.s("activation", gsil[:, ic * T + tsub * W: ic * T + (tsub + 1) * W],
                                     pup[sl][:], act, wait=[mm])
                            g_evs[(ic, tsub)] = ev
                            pfree[sl] = ev
                            lastev = ev
                        else:
                            hsl = tsub % 2
                            ev = b.v("tensor_tensor", hst[hsl][:],
                                     gsil[:, ic * T + tsub * W: ic * T + (tsub + 1) * W],
                                     pup[sl][:], ALU.mult,
                                     wait=[mm, g_evs[(ic, tsub)], hst_free[hsl]])
                            pfree[sl] = ev
                            G = (tsub * W) // TG
                            tloc = tsub * W - G * TG
                            base = ((e * 2 + G) * icc + ic) * 128 * TG
                            dst = h_scr[base: base + 128 * TG].rearrange("(p t) -> p t", p=128)[:, tloc:tloc + W]
                            hst_free[hsl] = b.dma("sync", dst, hst[hsl][:], wait=[ev])
                            lastev = hst_free[hsl]
                wfree = lastev

        b.barrier()
        b._scopes.pop()
        ab_stack.close()

        wdbuf = b.sbuf(f"wd{stage}", [128, icc * D], BF16)
        hbuf = b.sbuf(f"hb{stage}", [128, icc * TG], BF16)
        yacc = b.sbuf(f"yacc{stage}", [128, ntg * D], F32)
        pdn = [b.psum(f"pdn{stage}_{i}", [128, 512], F32) for i in range(2)]
        wd_d = H[pre + "wd"]
        scr2 = mk_ln_scratch(b, f"s{stage}post", D)
        xrt = b.sbuf(f"xrt{stage}", [128, D], F32)
        xlt = b.sbuf(f"xlt{stage}", [128, D], F32)
        h_store_done = [hst_free[0], hst_free[1]]
        wdfree = None
        hfree = None
        pfree = [None, None]
        xr_free = None
        xl_free = None
        out_free = None
        for G in range(2):
            ev_toks = {}
            for e in range(nE):
                wdtoks = []
                grp = 1 if icc % 2 else 2
                per = icc // grp
                for gg in range(grp):
                    s2 = wd_d[e * EW + gg * per * 128 * D: e * EW + (gg + 1) * per * 128 * D]
                    wdtoks.append(b.dma("gpsimd",
                                        wdbuf[:, gg * per * D:(gg + 1) * per * D].rearrange("p (ic n) -> p ic n", n=D),
                                        s2.rearrange("(ic p n) -> p ic n", p=128, n=D),
                                        wait=[wdfree] if gg == 0 else None))
                wdtok = wdtoks[-1]
                base = (e * 2 + G) * icc * 128 * TG
                htok = b.dma("sync", hbuf[:].rearrange("p (ic t) -> p ic t", t=TG),
                             h_scr[base: base + icc * 128 * TG].rearrange("(ic p t) -> p ic t", p=128, t=TG),
                             wait=[hfree] + h_store_done)
                mm_last = None
                for tl in range(ntg):
                    gtile = G * ntg + tl
                    for osub in range(4):
                        sl = (tl * 4 + osub) % 2
                        mm = None
                        for ic in range(icc):
                            w = None
                            if ic == 0:
                                w = [wdtok, htok, pfree[sl]]
                            mm = b.t_fin("matmul", pdn[sl][:],
                                         hbuf[:, ic * TG + tl * 128: ic * TG + tl * 128 + 128],
                                         wdbuf[:, ic * D + osub * 512: ic * D + (osub + 1) * 512],
                                         start=(ic == 0), stop=(ic == icc - 1), wait=w)
                        ccol = C_all[:, gtile * 9 + min(e, 8): gtile * 9 + min(e, 8) + 1]
                        ysl = yacc[:, tl * D + osub * 512: tl * D + (osub + 1) * 512]
                        prev = ev_toks.get((tl, osub), out_free if G else None)
                        if e == 0:
                            ev = b.v("tensor_scalar", ysl, pdn[sl][:], ccol, None, ALU.mult,
                                     wait=[mm, prev])
                        else:
                            ev = b.v("scalar_tensor_tensor", ysl, pdn[sl][:], ccol, ysl,
                                     ALU.mult, ALU.add, wait=[mm, prev])
                        ev_toks[(tl, osub)] = ev
                        pfree[sl] = ev
                        mm_last = ev
                wdfree = mm_last
                hfree = mm_last
            for tl in range(ntg):
                gtile = G * ntg + tl
                ysl = yacc[:, tl * D:(tl + 1) * D]
                deps = [ev_toks[(tl, o)] for o in range(4)]
                if stage == 4:
                    xl_tok = b.dma("sync", xlt[:], H["xln4"][gtile * 128:(gtile + 1) * 128, :],
                                   wait=[xl_free, xldma])
                    tadd = b.v("tensor_tensor", ysl, ysl, xlt[:], ALU.add, wait=deps + [xl_tok])
                    xl_free = tadd
                    deps = [tadd]
                tln = ln_tile(b, ysl, ysl, scr2, D, extra_tok=deps)
                xr_tok = b.dma("sync", xrt[:], x_in[gtile * 128:(gtile + 1) * 128, :], wait=[xr_free])
                tr = b.v("tensor_tensor", ysl, ysl, xrt[:], ALU.add, wait=[tln, xr_tok])
                xr_free = tr
                out_free = b.dma("sync", x_out[gtile * 128:(gtile + 1) * 128, :], ysl, wait=[tr])



def emit_stage3(b, H, cfg, ident3):
    T = cfg["T"]
    n_t = T // 128
    nch = n_t

    with b.scope():
        xt = [b.sbuf(f"x3t{i}", [128, D], F32) for i in range(2)]
        xbf = b.sbuf("x3bf", [128, D], BF16)
        fmst = [b.sbuf(f"x3fm{i}", [128, D], BF16) for i in range(2)]
        psbf = [b.psum(f"ps3t{i}", [128, 1024], BF16) for i in range(2)]
        scr = mk_ln_scratch(b, "s3pre", D)
        ldt = [None, None]
        fret = [None, None]
        odma = [None, None]
        for i in range(n_t):
            sl = i % 2
            ldt[sl] = b.dma("sync", xt[sl][:], H["x2"][i * 128:(i + 1) * 128, :], wait=[fret[sl]])
            tln = ln_tile(b, xbf[:], xt[sl][:], scr, D, extra_tok=[ldt[sl]])
            fret[sl] = tln
            evs = transpose_tm_to_fm(b, psbf, xbf,
                                     lambda half, sl=sl: fmst[sl][:, half * 1024:(half + 1) * 1024],
                                     ident3, [tln, odma[sl]])
            odma[sl] = b.dma("sync",
                             H["xlnf3"][i * 128 * D:(i + 1) * 128 * D].rearrange("(p f) -> p f", p=128),
                             fmst[sl][:], wait=evs)

    dec_stack = ExitStack()
    b._scopes.append(dec_stack)
    DT = b.sbuf("DT3", [128, NH * 128], BF16)
    kdec = b.sbuf("kdec3", [128, NH], F32)
    cdc = b.sbuf("cdc3", [128, NH], F32)
    qdecb = b.sbuf("qdecb3", [128, NH * 128], BF16)
    wcols = b.sbuf("wcols3", [128, 128], F32)
    S = b.sbuf("S3", [128, NH * 128], F32)
    Sb = b.sbuf("Sb3", [128, NH * 128], BF16)
    b._scopes.pop()

    with b.scope():
        ldcols = b.sbuf("ldcols", [128, NH], F32)
        t_lc = b.dma("sync", ldcols[:], H["s3_ld"][0:1, :].broadcast_to([128, NH]))
        iqb = b.sbuf("iqb", [128, 128], F32)
        t_iq = b.dma("sync", iqb[:], H["iq"][0:1, :].broadcast_to([128, 128]))
        ikc = b.sbuf("ikc", [128, 1], F32)
        t_ik = b.dma("sync", ikc[:], H["ikcol"][:, :])
        diffm = b.sbuf("diffm", [128, 128], F32)
        t_df = b.dma("sync", diffm[:], H["diffmat"][:, :])
        maskm = b.sbuf("maskm", [128, 128], F32)
        t_mk = b.dma("sync", maskm[:], H["maskmat"][:, :])
        dmb = b.sbuf("dmb", [128, 128], F32)
        t_dm = b.dma("sync", dmb[:], H["dmat128"][0:1, :].broadcast_to([128, 128]))
        mkb = b.sbuf("mkb", [128, 128], F32)
        t_m8 = b.dma("sync", mkb[:], H["mask128"][0:1, :].broadcast_to([128, 128]))
        tmpA = b.sbuf("tmpA3", [128, 128], F32)
        prev = None
        for h in range(NH):
            t1 = b.v("tensor_scalar", tmpA[:], diffm[:], ldcols[:, h:h + 1], None, ALU.mult,
                     wait=[t_df, t_lc] if h == 0 else [prev])
            t2 = b.s("activation", tmpA[:], tmpA[:], AF.Exp, wait=[t1])
            prev = b.v("tensor_tensor", DT[:, h * 128:(h + 1) * 128], tmpA[:], maskm[:], ALU.mult,
                       wait=[t2, t_mk])
            t3 = b.v("tensor_scalar", tmpA[:], iqb[:], ldcols[:, h:h + 1], None, ALU.mult,
                     wait=[prev, t_iq])
            prev = b.s("activation", qdecb[:, h * 128:(h + 1) * 128], tmpA[:], AF.Exp, wait=[t3])
        tk_tmp = b.sbuf("kdtmp", [128, NH], F32)
        tk2 = b.v("tensor_scalar", tk_tmp[:], ldcols[:], ikc[:, 0:1], None, ALU.mult,
                  wait=[t_lc, t_ik])
        tk3 = b.s("activation", kdec[:], tk_tmp[:], AF.Exp, wait=[tk2])
        tc = b.v("tensor_scalar", cdc[:], ldcols[:], 128.0, None, ALU.mult, wait=[t_lc])
        tc = b.s("activation", cdc[:], cdc[:], AF.Exp, wait=[tc])
        wt1 = b.v("tensor_tensor", wcols[:].rearrange("p (h j) -> p h j", j=8),
                  ldcols[:].unsqueeze(2).broadcast_to([128, NH, 8]),
                  dmb[:].rearrange("p (h j) -> p h j", j=8), ALU.mult, wait=[t_lc, t_dm, prev])
        wt2 = b.s("activation", wcols[:], wcols[:], AF.Exp, wait=[wt1])
        b.v("tensor_tensor", wcols[:], wcols[:], mkb[:], ALU.mult, wait=[wt2, t_m8])

    def proj3_fm(wname, out_scr):
        with b.scope():
            wsb = b.sbuf("w3_" + wname, [128, NCK * D], BF16)
            wtok = load_weight_split(b, wsb, H[wname][:, :], NCK, D, groups=4)
            fmt = [b.sbuf(f"fm3{wname}{i}", [128, D], BF16) for i in range(2)]
            stg = [b.sbuf(f"stg3{wname}{i}", [128, D], BF16) for i in range(2)]
            pp = [b.psum(f"pp3{wname}{i}", [128, 512], F32) for i in range(2)]
            ldt = [None, None]
            fret = [None, None]
            pfr = [None, None]
            sfree = [None, None]
            for i in range(n_t):
                sl = i % 2
                ldt[sl] = b.dma("sync", fmt[sl][:],
                                H["xlnf3"][i * 128 * D:(i + 1) * 128 * D].rearrange("(p f) -> p f", p=128),
                                wait=[fret[sl]])
                evs = []
                for h in range(NH):
                    psl = h % 2
                    mm = None
                    for kc in range(NCK):
                        w = None
                        if kc == 0:
                            w = [ldt[sl], wtok, pfr[psl]]
                        mm = b.t_fin("matmul", pp[psl][:, 0:128],
                                     wsb[:, kc * D + h * 128: kc * D + h * 128 + 128],
                                     fmt[sl][:, kc * 128:(kc + 1) * 128],
                                     start=(kc == 0), stop=(kc == NCK - 1), wait=w)
                    ev = b.s("activation", stg[sl][:, h * 128:(h + 1) * 128], pp[psl][:, 0:128],
                             AF.Silu, wait=[mm, sfree[sl] if h == 0 else None])
                    pfr[psl] = ev
                    evs.append(ev)
                fret[sl] = evs[-1]
                sfree[sl] = b.dma("sync",
                                  out_scr[i * 128 * D:(i + 1) * 128 * D].rearrange("(p f) -> p f", p=128),
                                  stg[sl][:], wait=evs)

    proj3_fm("s3_Wq", H["q3"])
    proj3_fm("s3_Wk", H["k3"])

    with b.scope():
        wsb = b.sbuf("w3v", [128, NCK * D], BF16)
        wtok = load_weight_split(b, wsb, H["s3_Wv"][:, :], NCK, D, groups=4)
        wg1 = b.sbuf("w3g1", [128, NCK * 128], BF16)
        wg1tok = b.dma("gpsimd", wg1[:].rearrange("p (kc n) -> p kc n", n=128),
                        H["s3_Wg1"][:, :].rearrange("(kc p) n -> p kc n", p=128))
        wg2 = b.sbuf("w3g2", [128, D], BF16)
        wg2tok = b.dma("gpsimd", wg2[:], H["s3_Wg2"][:, :])
        nwc = b.sbuf("nwc3", [128, NCK], F32)
        nwtok = b.dma("sync", nwc[:], H["s3_normw_cols"][:, :])
        fmt = [b.sbuf(f"fm3v{i}", [128, D], BF16) for i in range(2)]
        g1 = b.sbuf("g1_3", [128, T], BF16)
        pp = b.psum("pp3v", [128, D], F32)
        ppg = b.psum("ppg3", [128, 512], F32)
        stg = [b.sbuf(f"stg3v{i}", [128, D], BF16) for i in range(2)]
        gst = b.sbuf("gst3", [128, D], F32)
        gbf = b.sbuf("gbf3", [128, D], BF16)
        ldt = [None, None]
        fret = [None, None]
        ppfree = None
        gfree = None
        vdma = [None, None]
        for i in range(n_t):
            sl = i % 2
            ldt[sl] = b.dma("sync", fmt[sl][:],
                            H["xlnf3"][i * 128 * D:(i + 1) * 128 * D].rearrange("(p f) -> p f", p=128),
                            wait=[fret[sl]])
            mm = None
            for nsub in range(4):
                for kc in range(NCK):
                    w = None
                    if nsub == 0 and kc == 0:
                        w = [ldt[sl], wtok, ppfree]
                    mm = b.t_fin("matmul", pp[:, nsub * 512:(nsub + 1) * 512],
                                 fmt[sl][:, kc * 128:(kc + 1) * 128],
                                 wsb[:, kc * D + nsub * 512: kc * D + (nsub + 1) * 512],
                                 start=(kc == 0), stop=(kc == NCK - 1), wait=w)
            ev = b.s("activation", stg[sl][:], pp[:], AF.Silu, wait=[mm, vdma[sl]])
            ppfree = ev
            vdma[sl] = b.dma("sync", H["v3"][i * 128:(i + 1) * 128, :], stg[sl][:], wait=[ev])
            mmg = None
            for kc in range(NCK):
                w = None
                if kc == 0:
                    w = [ldt[sl], wg1tok, gfree]
                mmg = b.t_fin("matmul", ppg[:, 0:128],
                              wg1[:, kc * 128:(kc + 1) * 128],
                              fmt[sl][:, kc * 128:(kc + 1) * 128],
                              start=(kc == 0), stop=(kc == NCK - 1), wait=w)
            gfree = b.v("tensor_copy", g1[:, i * 128:(i + 1) * 128], ppg[:, 0:128], wait=[mmg])
            fret[sl] = gfree
        ppfree2 = ppfree
        gdma = None
        for i in range(n_t):
            mm = None
            for nsub in range(4):
                w = None
                if nsub == 0:
                    w = [gfree, wg2tok, ppfree2]
                mm = b.t_fin("matmul", pp[:, nsub * 512:(nsub + 1) * 512],
                             g1[:, i * 128:(i + 1) * 128],
                             wg2[:, nsub * 512:(nsub + 1) * 512],
                             start=True, stop=True, wait=w)
            t1 = b.s("activation", gbf[:], pp[:], AF.Sigmoid, wait=[mm, gdma])
            ppfree2 = t1
            gdma = b.dma("sync", H["gate3"][i * 128:(i + 1) * 128, :], gbf[:], wait=[t1])

    with b.scope():
        kb = [b.sbuf(f"kb3S{i}", [128, D], BF16) for i in range(2)]
        vb = [b.sbuf(f"vb3S{i}", [128, D], BF16) for i in range(2)]
        pT = b.psum("pT3S", [128, 128], BF16)
        pC = b.psum("pC3S", [128, 128], F32)
        kd = b.sbuf("kd3S", [128, 128], BF16)
        zinit = b.v("memset", S[:], 0.0)
        ldk = [None, None]
        ldv = [None, None]
        fre = [None, None]
        s_toks = {h: zinit for h in range(NH)}
        kd_free = None
        pT_free = None
        pC_free = None
        for c in range(nch):
            sl = c % 2
            ldk[sl] = b.dma("sync", kb[sl][:],
                            H["k3"][c * 128 * D:(c + 1) * 128 * D].rearrange("(p f) -> p f", p=128),
                            wait=[fre[sl]])
            ldv[sl] = b.dma("sync", vb[sl][:], H["v3"][c * 128:(c + 1) * 128, :], wait=[fre[sl]])
            last = None
            for h in range(NH):
                ttok = b.t_fin("transpose", pT[:], kb[sl][:, h * 128:(h + 1) * 128], ident3[:],
                               wait=[ldk[sl], pT_free])
                t1 = b.v("tensor_scalar", kd[:], pT[:], kdec[:, h:h + 1], None, ALU.mult,
                         wait=[ttok, kd_free])
                pT_free = t1
                ctok = b.t_fin("matmul", pC[:], kd[:], vb[sl][:, h * 128:(h + 1) * 128],
                               start=True, stop=True, wait=[t1, ldv[sl], pC_free])
                kd_free = ctok
                s_toks[h] = b.v("scalar_tensor_tensor", S[:, h * 128:(h + 1) * 128],
                                S[:, h * 128:(h + 1) * 128], cdc[:, h:h + 1], pC[:],
                                ALU.mult, ALU.add, wait=[ctok, s_toks[h]])
                pC_free = s_toks[h]
                last = s_toks[h]
            fre[sl] = last
        stok = b.dma("sync", H["ccin"][:, :].rearrange("(h d) e -> d h e", d=128),
                     S[:].rearrange("p (h e) -> p h e", e=128),
                     wait=[s_toks[h] for h in range(NH)])
        if DEBUG_NO_CC:
            cctok = stok
            for jp in range(NCORES):
                cctok = b.dma("gpsimd", H["ccout"][jp * NH * 128:(jp + 1) * NH * 128, :],
                              H["ccin"][:, :], wait=[stok])
        else:
            ccsem = b.sem("ccsem")
            cctok = b.op("gpsimd", "collective_compute", "AllGather", ALU.bypass,
                         [list(range(NCORES))], [H["ccin"][:, :]], [H["ccout"][:, :]],
                         wait=[stok], inc=[(ccsem, 1)])
        gbufs = [b.sbuf(f"gb3{i}", [128, 128], F32) for i in range(2)]
        gtoks = [None, None]
        stoks = {}
        for h in range(NH):
            for jp in range(NCORES):
                sl = (h * NCORES + jp) % 2
                roff = jp * (NH * 128) + h * 128
                gtoks[sl] = b.dma("gpsimd", gbufs[sl][:], H["ccout"][roff:roff + 128, :],
                                  wait=[cctok, gtoks[sl]])
                wc = wcols[:, h * 8 + jp: h * 8 + jp + 1]
                if jp == 0:
                    stoks[h] = b.v("tensor_scalar", S[:, h * 128:(h + 1) * 128], gbufs[sl][:],
                                   wc, None, ALU.mult, wait=[gtoks[sl], stok, s_toks[h]])
                else:
                    stoks[h] = b.v("scalar_tensor_tensor", S[:, h * 128:(h + 1) * 128], gbufs[sl][:],
                                   wc, S[:, h * 128:(h + 1) * 128], ALU.mult, ALU.add,
                                   wait=[gtoks[sl], stoks[h]])
                gtoks[sl] = stoks[h]
        sb_tok = b.v("tensor_copy", Sb[:], S[:], wait=[stoks[h] for h in range(NH)])

    with b.scope():
        kb = [b.sbuf(f"kb3{i}", [128, D], BF16) for i in range(2)]
        qb = [b.sbuf(f"qb3{i}", [128, D], BF16) for i in range(2)]
        vb = [b.sbuf(f"vb3{i}", [128, D], BF16) for i in range(2)]
        gb = [b.sbuf(f"gb3b{i}", [128, D], BF16) for i in range(2)]
        x2t = [b.sbuf(f"x2t{i}", [128, D], F32) for i in range(2)]
        wo = b.sbuf("wo3", [128, NCK * D], BF16)
        wotok0 = load_weight_split(b, wo, H["s3_Wo"][:, :], NCK, D, groups=4)
        nwc2 = b.sbuf("nwc3b", [128, NCK], F32)
        nwtok2 = b.dma("sync", nwc2[:], H["s3_normw_cols"][:, :])
        wotok = wotok0
        for kc in range(NCK):
            wotok = b.v("tensor_scalar", wo[:, kc * D:(kc + 1) * D], wo[:, kc * D:(kc + 1) * D],
                        nwc2[:, kc:kc + 1], None, ALU.mult,
                        wait=[wotok0, nwtok2] if kc == 0 else [wotok])
        psc = b.psum("psc3", [128, 128], F32)
        pT = b.psum("pT3", [128, 128], BF16)
        pC = b.psum("pC3", [128, 128], F32)
        po = b.psum("po3", [128, D], F32)
        psb2 = b.psum("psb23", [128, 1024], BF16)
        smb = b.sbuf("smb3", [128, 128], BF16)
        qd = b.sbuf("qd3", [128, 128], BF16)
        kd2 = b.sbuf("kd23", [128, 128], BF16)
        otile = b.sbuf("ot3", [128, D], F32)
        sq3 = b.sbuf("sq3s", [128, D], F32)
        ssum = b.sbuf("ssum3", [128, NH], F32)
        onb = b.sbuf("onb3", [128, D], BF16)
        onf = b.sbuf("onf3", [128, D], BF16)
        out3 = b.sbuf("out3s", [128, D], F32)
        scr = mk_ln_scratch(b, "s3b", D)
        ldq = [None, None]
        ldk = [None, None]
        ldv = [None, None]
        ldg = [None, None]
        ldx = [None, None]
        fre = [None, None]
        sc_free = None
        smb_free = None
        qd_free = None
        kd2_free = None
        pT_free = None
        pC_free = None
        po_free = None
        psb2_free = None
        out_dma = None
        s_upd = {h: sb_tok for h in range(NH)}
        sb_cast = {h: sb_tok for h in range(NH)}
        for c in range(nch):
            sl = c % 2
            wl = [fre[sl]]
            ldq[sl] = b.dma("sync", qb[sl][:],
                            H["q3"][c * 128 * D:(c + 1) * 128 * D].rearrange("(p f) -> p f", p=128), wait=wl)
            ldk[sl] = b.dma("sync", kb[sl][:],
                            H["k3"][c * 128 * D:(c + 1) * 128 * D].rearrange("(p f) -> p f", p=128), wait=wl)
            ldv[sl] = b.dma("sync", vb[sl][:], H["v3"][c * 128:(c + 1) * 128, :], wait=wl)
            ldg[sl] = b.dma("sync", gb[sl][:], H["gate3"][c * 128:(c + 1) * 128, :], wait=wl)
            ldx[sl] = b.dma("sync", x2t[sl][:], H["x2"][c * 128:(c + 1) * 128, :], wait=wl)
            bank_last = {}
            for h in range(NH):
                hs = h * 128
                mm1 = b.t_fin("matmul", psc[:], kb[sl][:, hs:hs + 128], qb[sl][:, hs:hs + 128],
                              start=True, stop=True, wait=[ldk[sl], ldq[sl], sc_free])
                tsm = b.v("tensor_tensor", smb[:], psc[:], DT[:, hs:hs + 128], ALU.mult,
                          wait=[mm1, smb_free])
                sc_free = tsm
                tqd = b.v("tensor_tensor", qd[:], qb[sl][:, hs:hs + 128], qdecb[:, hs:hs + 128],
                          ALU.mult, wait=[ldq[sl], qd_free])
                bank = h // 4
                col0 = bank * 512 + (h % 4) * 128
                w1 = [tsm, ldv[sl]]
                if h % 4 == 0:
                    w1.append(po_free)
                mo1 = b.t_fin("matmul", po[:, col0:col0 + 128], smb[:], vb[sl][:, hs:hs + 128],
                              start=(h % 4 == 0), stop=False, wait=w1)
                smb_free = mo1
                mo2 = b.t_fin("matmul", po[:, col0:col0 + 128], qd[:], Sb[:, hs:hs + 128],
                              start=False, stop=(h % 4 == 3), wait=[tqd, sb_cast[h]])
                qd_free = mo2
                bank_last[bank] = mo2
                tT = b.t_fin("transpose", pT[:], kb[sl][:, hs:hs + 128], ident3[:],
                             wait=[ldk[sl], pT_free])
                tkd = b.v("tensor_scalar", kd2[:], pT[:], kdec[:, h:h + 1], None, ALU.mult,
                          wait=[tT, kd2_free])
                pT_free = tkd
                mmc = b.t_fin("matmul", pC[:], kd2[:], vb[sl][:, hs:hs + 128],
                              start=True, stop=True, wait=[tkd, ldv[sl], pC_free])
                kd2_free = mmc
                s_upd[h] = b.v("scalar_tensor_tensor", S[:, hs:hs + 128], S[:, hs:hs + 128],
                               cdc[:, h:h + 1], pC[:], ALU.mult, ALU.add,
                               wait=[mmc, s_upd[h], sb_cast[h]])
                pC_free = s_upd[h]
                sb_cast[h] = b.v("tensor_copy", Sb[:, hs:hs + 128], S[:, hs:hs + 128],
                                 wait=[s_upd[h], mo2])
                if "dbg_qd" in H and c == 1 and h == 0:
                    b.dma("sync", H["dbg_qd"][:, :], qd[:], wait=[tqd])
                    b.dma("sync", H["dbg_sm"][:, :], smb[:], wait=[tsm])
            if "dbg_S0" in H and c == 0:
                b.dma("sync", H["dbg_S0"][:, :], S[:], wait=[s_upd[h2] for h2 in range(NH)])
            evs = []
            for bank in range(4):
                evs.append(b.s("activation", otile[:, bank * 512:(bank + 1) * 512],
                               po[:, bank * 512:(bank + 1) * 512], AF.Copy, wait=[bank_last[bank]]))
            if "dbg_o3" in H:
                dbg_t = b.dma("sync", H["dbg_o3"][c * 128:(c + 1) * 128, :], otile[:], wait=evs)
                evs = evs + [dbg_t]
            t1 = b.v("tensor_tensor", otile[:], otile[:], gb[sl][:], ALU.mult, wait=evs + [ldg[sl]])
            t2 = b.s("activation", sq3[:], otile[:], AF.Square, wait=[t1])
            t3 = b.v("tensor_reduce", ssum[:], sq3[:].rearrange("p (h d) -> p h d", d=128), AX,
                     ALU.add, wait=[t2])
            t4 = b.v("tensor_scalar", ssum[:], ssum[:], 1.0 / 128.0, 1e-5, ALU.mult, ALU.add, wait=[t3])
            t5 = b.s("activation", ssum[:], ssum[:], AF.Sqrt, wait=[t4])
            t6 = b.v("reciprocal", ssum[:], ssum[:], wait=[t5])
            t7 = b.v("tensor_tensor", onb[:].rearrange("p (h d) -> p h d", d=128),
                     otile[:].rearrange("p (h d) -> p h d", d=128),
                     ssum[:].unsqueeze(2).broadcast_to([128, NH, 128]), ALU.mult, wait=[t6])
            if "dbg_on3" in H:
                t7 = b.dma("sync", H["dbg_on3"][c * 128:(c + 1) * 128, :], onb[:], wait=[t7]) and t7
            evs2 = []
            for half in range(2):
                mm = None
                for jj in range(8):
                    cc = half * 8 + jj
                    w = None
                    if jj == 0:
                        w = [t7, psb2_free if half == 0 else evs2[0], po_free if half == 0 else None]
                    mm = b.t_fin("transpose", psb2[:, jj * 128:(jj + 1) * 128],
                                 onb[:, cc * 128:(cc + 1) * 128], ident3[:], wait=w)
                evs2.append(b.v("tensor_copy", onf[:, half * 1024:(half + 1) * 1024], psb2[:], wait=[mm]))
            psb2_free = evs2[-1]
            mmf = None
            for nsub in range(4):
                for kc in range(NCK):
                    w = None
                    if nsub == 0 and kc == 0:
                        w = evs2 + [wotok] + evs
                    mmf = b.t_fin("matmul", po[:, nsub * 512:(nsub + 1) * 512],
                                  onf[:, kc * 128:(kc + 1) * 128],
                                  wo[:, kc * D + nsub * 512: kc * D + (nsub + 1) * 512],
                                  start=(kc == 0), stop=(kc == NCK - 1), wait=w)
            tcp = b.s("activation", out3[:], po[:], AF.Copy, wait=[mmf, out_dma])
            po_free = tcp
            tln = ln_tile(b, out3[:], out3[:], scr, D, extra_tok=[tcp])
            tr = b.v("tensor_tensor", out3[:], out3[:], x2t[sl][:], ALU.add, wait=[tln, ldx[sl]])
            fre[sl] = tr
            out_dma = b.dma("sync", H["x3"][c * 128:(c + 1) * 128, :], out3[:], wait=[tr])

    dec_stack.close()



def declare_dram(nc, name, shape, dtype, kind):
    return nc.dram_tensor(name, list(shape), dtype, kind=kind).ap()


def build_stage_nc(stage, cfg):
    T = cfg["T"]
    Text = HALO_L + T + HALO_R
    F = cfg["F"]
    FS = cfg["FS"]
    nc = bass.Bass("TRN2", target_bir_lowering=False)
    ctx = ExitStack()
    b = BB(nc, ctx)
    H = {}

    def ext_in(name, shape, dtype=F32):
        H[name] = declare_dram(nc, name, shape, dtype, "ExternalInput")

    def ext_out(name, shape, dtype=F32):
        H[name] = declare_dram(nc, name, shape, dtype, "ExternalOutput")

    def internal(name, shape, dtype):
        H[name] = declare_dram(nc, name, shape, dtype, "Internal")

    ext_in("ident", [128, 128], BF16)
    if stage == 1:
        ext_in("x_ext", [Text, D])
        for nm in ("wqT", "wkT", "wvT", "woT"):
            ext_in("s1_" + nm, [D, D])
        for nm in ("bq", "bk", "bv", "bo", "gq", "gk"):
            ext_in("s1_" + nm, [D])
        ext_in("s1_wmix", [1, 5])
        ext_in("ident_rep", [128, D], BF16)
        internal("xlnf1", [Text * D], BF16)
        internal("k1", [Text, D], BF16)
        internal("v1", [Text, D], BF16)
        internal("q1", [T, D], BF16)
        internal("ao1", [T, D], BF16)
        ext_out("x1", [T, D])
        emit_stage1(b, H, cfg)
    elif stage in (2, 4):
        nE = (8 + FS // F) if stage == 2 else 9
        pre = f"s{stage}_"
        ext_in("x1" if stage == 2 else "x3", [T, D])
        ext_in(pre + "wg", [nE * D * F])
        ext_in(pre + "wu", [nE * D * F])
        ext_in(pre + "wd", [nE * D * F])
        ext_in(pre + "rw9", [D, 9])
        internal(f"h{stage}", [nE * 2 * (F // 128) * 128 * (T // 2)], BF16)
        if stage == 4:
            internal("xln4", [T, D], F32)
        ext_out("x2" if stage == 2 else "xout", [T, D])
        emit_moe(b, H, cfg, stage)
    elif stage == 3:
        ext_in("x2", [T, D])
        for nm in ("Wq", "Wk", "Wv", "Wo"):
            ext_in("s3_" + nm, [D, D])
        ext_in("s3_Wg1", [D, 128])
        ext_in("s3_Wg2", [128, D])
        ext_in("s3_normw_cols", [128, NCK])
        ext_in("s3_ld", [1, NH])
        ext_in("iq", [1, 128])
        ext_in("ik", [1, 128])
        ext_in("ikcol", [128, 1])
        ext_in("diffmat", [128, 128])
        ext_in("maskmat", [128, 128])
        ext_in("dmat128", [1, 128])
        ext_in("mask128", [1, 128])
        internal("xlnf3", [T * D], BF16)
        internal("q3", [T * D], BF16)
        internal("k3", [T * D], BF16)
        internal("v3", [T, D], BF16)
        internal("gate3", [T, D], BF16)
        internal("ccin", [NH * 128, 128], F32)
        internal("ccout", [NCORES * NH * 128, 128], F32)
        internal("ccbar", [16, 128], F32)
        internal("ccbar_o", [16, 128], F32)
        ext_out("x3", [T, D])
        if DEBUG_S3:
            for nm, shp, dt_ in (("dbg_q3", [T * D], BF16), ("dbg_k3", [T * D], BF16),
                                 ("dbg_v3", [T, D], BF16), ("dbg_gate3", [T, D], BF16),
                                 ("dbg_ccin", [NH * 128, 128], F32),
                                 ("dbg_o3", [T, D], F32),
                                 ("dbg_S0", [128, NH * 128], F32),
                                 ("dbg_qdecb", [128, NH * 128], BF16),
                                 ("dbg_qd", [128, 128], BF16),
                                 ("dbg_sm", [128, 128], BF16),
                                 ("dbg_on3", [T, D], BF16),
                                 ("dbg_xlnf3", [T * D], BF16)):
                H[nm] = declare_dram(nc, nm, shp, dt_, "ExternalOutput")
        ident3 = b.sbuf("ident3b", [128, 128], BF16)
        b.dma("sync", ident3[:], H["ident"][:, :])
        b.barrier()
        emit_stage3(b, H, cfg, ident3)
        if DEBUG_S3:
            for nm in ("q3", "k3", "v3", "gate3", "ccin", "xlnf3"):
                b.dma("gpsimd", H["dbg_" + nm][:, :] if H["dbg_" + nm].ndim == 2 else H["dbg_" + nm][:],
                      H[nm][:, :] if H[nm].ndim == 2 else H[nm][:])
    b.barrier()
    with nc.Block() as block:
        b.finalize(block)
    ctx.close()
    return nc


def stage_input_names(stage):
    names = ["ident"]
    if stage == 1:
        names += ["x_ext", "ident_rep", "s1_wqT", "s1_wkT", "s1_wvT", "s1_woT",
                  "s1_bq", "s1_bk", "s1_bv", "s1_bo", "s1_gq", "s1_gk", "s1_wmix"]
    elif stage in (2, 4):
        pre = f"s{stage}_"
        names += [pre + "wg", pre + "wu", pre + "wd", pre + "rw9",
                  "x1" if stage == 2 else "x3"]
    elif stage == 3:
        names += ["x2", "s3_Wq", "s3_Wk", "s3_Wv", "s3_Wo", "s3_Wg1", "s3_Wg2",
                  "s3_normw_cols", "s3_ld", "iq", "ik", "ikcol", "diffmat", "maskmat", "dmat128", "mask128"]
    return names


_NC_CACHE = {}


def run_stages(inputs, cfg, stages=(1, 2, 3, 4), carry_in=None, collect=False):
    sh, cores = prep_host(inputs, cfg)
    carry = carry_in
    outs = {}
    for stage in stages:
        key = (stage, cfg["T"], cfg["F"], cfg["FS"])
        if key not in _NC_CACHE:
            _NC_CACHE[key] = build_stage_nc(stage, cfg)
        nc = _NC_CACHE[key]
        names = stage_input_names(stage)
        in_maps = []
        for j in range(NCORES):
            m = {}
            for nm in names:
                if nm in ("x1", "x2", "x3"):
                    m[nm] = np.ascontiguousarray(carry[j])
                else:
                    m[nm] = cores[j][nm]
            in_maps.append(m)
        res = run_bass_kernel_spmd(nc, in_maps, core_ids=list(range(NCORES)))
        okey = {1: "x1", 2: "x2", 3: "x3", 4: "xout"}[stage]
        carry = [np.asarray(res.results[j][okey]) for j in range(NCORES)]
        outs[stage] = carry
    return outs


def assemble(final, cfg):
    T, L = cfg["T"], cfg["L"]
    out = np.zeros((2, L, D), np.float32)
    for j in range(NCORES):
        bidx, slot = j // CPB, j % CPB
        out[bidx, slot * T:(slot + 1) * T, :] = final[j]
    return out


def kernel(**inputs):
    cfg = CFG_FULL
    outs = run_stages(inputs, cfg, stages=(1, 2, 3, 4))
    return assemble(outs[4], cfg)
